# version 13 (final)
# speedup vs baseline: 2.4935x; 1.0017x over previous
import math
from contextlib import ExitStack

import numpy as np
import ml_dtypes

import concourse.bass as bass
import concourse.mybir as mybir
from concourse.bass_utils import run_bass_kernel_spmd

F32 = mybir.dt.float32
BF16 = mybir.dt.bfloat16
I32 = mybir.dt.int32
AF = mybir.ActivationFunctionType
ALU = mybir.AluOpType
AX = mybir.AxisListType
BF = ml_dtypes.bfloat16

NEG_BIG = -1.0e30


def default_cfg():
    return dict(
        S=4096,
        D=1024,
        H=16,
        HD=64,
        LAT=512,
        F0=2730,
        NG=4,
        E0=8,
        E1=8,
        HID1=4096,
        R=8,
        TB=512,
        FTB=512,
        NTOK=8 * 4096,
        EPS=1e-5,
        ROPE_THETA=10000.0,
    )



class Prog:

    def __init__(self, name="prog"):
        self.nc = bass.Bass("TRN2", target_bir_lowering=False)
        self.stack = ExitStack()
        self.sem_vals = {}
        self.sems = {}
        self.tensors = {}
        self.prefix = ""
        self._scopes = []

    def sem(self, name):
        if name not in self.sems:
            h = self.stack.enter_context(self.nc.semaphore(name))
            self.sems[name] = h
            self.sem_vals[name] = 0
        return self.sems[name]

    def inc(self, inst, name, n=1):
        inst.then_inc(self.sem(name), n)
        self.sem_vals[name] += n
        return self.sem_vals[name]

    def val(self, name):
        self.sem(name)
        return self.sem_vals[name]

    def wait(self, eng, name, val=None):
        eng.wait_ge(self.sem(name), self.sem_vals[name] if val is None else val)

    def _alloc_stack(self):
        return self._scopes[-1] if self._scopes else self.stack

    def sbuf(self, name, shape, dtype):
        h = self._alloc_stack().enter_context(
            self.nc.sbuf_tensor(self.prefix + name, shape, dtype))
        self.tensors[name] = h
        return h

    def psum(self, name, shape, dtype=F32):
        h = self._alloc_stack().enter_context(
            self.nc.psum_tensor(self.prefix + name, shape, dtype))
        self.tensors[name] = h
        return h

    def push_scope(self, prefix):
        self.prefix = prefix
        self._scopes.append(ExitStack())

    def pop_scope(self):
        self._scopes.pop().close()
        self.prefix = ""

    def barrier(self):
        totals = dict(self.sem_vals)
        for en in ("pe", "dve", "act", "gp"):
            e = self.eng(en)
            for s, v in totals.items():
                if s == e.semname or v == 0:
                    continue
                e.after(s, v)
            e.op("nop")
        for s, v in totals.items():
            if v:
                self.nc.sync.wait_ge(self.sem(s), v)

    def din(self, name, shape, dtype):
        ap = self.nc.dram_tensor(name, shape, dtype, kind="ExternalInput").ap()
        self.tensors[name] = ap
        return ap

    def dout(self, name, shape, dtype):
        ap = self.nc.dram_tensor(name, shape, dtype, kind="ExternalOutput").ap()
        self.tensors[name] = ap
        return ap

    def dtmp(self, name, shape, dtype, addr_space="Local"):
        ap = self.nc.dram_tensor(self.prefix + name, shape, dtype,
                                 kind="Internal", addr_space=addr_space).ap()
        self.tensors[name] = ap
        return ap

    def dma(self, eng, out, in_, sem_name, **kw):
        inst = eng.dma_start(out, in_, **kw)
        return self.inc(inst, sem_name, 16)

    def eng(self, name):
        if not hasattr(self, "_engs"):
            self._engs = {}
        if name not in self._engs:
            raw = {"pe": self.nc.tensor, "dve": self.nc.vector,
                   "act": self.nc.scalar, "gp": self.nc.gpsimd,
                   "sp": self.nc.sync}[name]
            self._engs[name] = Eng(self, raw, name)
        return self._engs[name]

    def chains(self):
        return (self.eng("pe"), self.eng("dve"), self.eng("act"),
                self.eng("gp"), self.eng("sp"))

    def finish(self):
        self.stack.close()
        return self.nc


class Eng:

    def __init__(self, p: Prog, raw, name):
        self.p = p
        self.raw = raw
        self.name = name
        self.semname = name + "_tick"
        self.t = 0
        self._pending = []

    def after(self, other, tick=None):
        if isinstance(other, Eng):
            self._pending.append((other.semname, other.t if tick is None else tick))
        else:
            self._pending.append((other, tick))
        return self

    def op(self, fname, *a, **kw):
        for semname, v in self._pending:
            self.raw.wait_ge(self.p.sem(semname), v)
        self._pending.clear()
        if self.t and self.name != "pe":
            self.raw.wait_ge(self.p.sem(self.semname), self.t)
        inst = getattr(self.raw, fname)(*a, **kw)
        self.t = self.p.inc(inst, self.semname)
        return inst

    def dma(self, out, in_, sem_name, **kw):
        for semname, v in self._pending:
            self.raw.wait_ge(self.p.sem(semname), v)
        self._pending.clear()
        inst = self.raw.dma_start(out, in_, **kw)
        return self.p.inc(inst, sem_name, 16)

    def wait_dma(self, sem_name, val=None):
        self._pending.append((sem_name, self.p.val(sem_name) if val is None else val))
        return self


def declare_consts(p: Prog):
    p.din("c_ones_col", [128, 1], F32)
    p.din("c_ones_row", [1, 128], F32)
    p.din("c_one1", [1, 1], F32)
    p.din("c_ones_row2", [2, 128], F32)
    p.din("c_ones_col_bf", [128, 1], BF16)


def const_arrays():
    return {
        "c_ones_col": np.ones((128, 1), np.float32),
        "c_ones_row": np.ones((1, 128), np.float32),
        "c_one1": np.ones((1, 1), np.float32),
        "c_ones_row2": np.ones((2, 128), np.float32),
        "c_ones_col_bf": np.ones((128, 1), BF),
    }



def emit_ln_router(p: Prog, cfg, layer: int):
    nc = p.nc
    pe, dve, act, gp, sp = p.chains()
    S, D, TB = cfg["S"], cfg["D"], cfg["TB"]
    E = cfg["E0"] if layer == 0 else cfg["E1"]
    NC = D // 128
    NTB = S // TB
    NTT = TB // 128

    x = p.tensors["x_fm"]
    wr = p.tensors["w_r"]
    br = p.tensors["b_r"]
    xn = p.tensors["xn_fm"]
    rout = p.tensors["probs" if layer == 0 else "combine"]

    ones_col = p.tensors["c_ones_col"]
    ones_row = p.tensors["c_ones_row"]
    one1 = p.tensors["c_one1"]

    xb = p.sbuf("lr_x", [128, NC * TB], F32)
    xsq = p.sbuf("lr_xsq", [128, NC * TB], F32)
    wrb = p.sbuf("lr_wr", [128, NC * E], F32)
    csbr = p.sbuf("lr_csbr", [2, E], F32)
    brow2 = p.sbuf("lr_brow2", [2, 128], F32)
    onc = p.sbuf("lr_onc", [128, 1], F32)
    onr = p.sbuf("lr_onr", [1, 128], F32)
    on1 = p.sbuf("lr_on1", [1, 1], F32)
    rows = p.sbuf("lr_rows", [1, 8 * TB], F32)
    ab_sb = p.sbuf("lr_absb", [128, 2 * TB], F32)
    xnb = p.sbuf("lr_xn", [128, NC * TB], BF16)
    tmp = p.sbuf("lr_tmp", [128, NC * TB], F32)
    acol = p.sbuf("lr_acol", [128, 1], F32)
    bc8 = p.sbuf("lr_bc8", [128, E], F32)
    lg = p.sbuf("lr_lg", [128, 6 * E + 8], F32)

    ps_sx = p.psum("lr_ps_sx", [1, TB])
    ps_sq = p.psum("lr_ps_sq", [1, TB])
    ps_ab = p.psum("lr_ps_ab", [128, 2 * TB])
    ps_lg = p.psum("lr_ps_lg", [128, E])
    ps_c1 = p.psum("lr_ps_c1", [128, 1])
    ps_c8 = p.psum("lr_ps_c8", [128, E])
    ps_cs = p.psum("lr_ps_cs", [1, E])

    sp.dma(onc[:], ones_col, "ld")
    sp.dma(onr[:], ones_row, "ld")
    sp.dma(on1[:], one1, "ld")
    sp.dma(brow2[1:2, :], ones_row, "ld")
    sp.dma(csbr[1:2, :], br, "ld")
    for c in range(NC):
        sp.dma(wrb[:, c * E:(c + 1) * E], wr[c * 128:(c + 1) * 128, :], "ld")
    pe.wait_dma("ld")
    for c in range(NC):
        pe.op("matmul", ps_cs[:], onc[:], wrb[:, c * E:(c + 1) * E],
              start=(c == 0), stop=(c == NC - 1))
    dve.after(pe).op("tensor_copy", csbr[0:1, :], ps_cs[:])

    for tb in range(NTB):
        t0 = tb * TB
        sp.after(pe).after(act).after(gp).after(dve)
        for c in range(NC):
            sp.dma(xb[:, c * TB:(c + 1) * TB],
                   x[c * 128:(c + 1) * 128, t0:t0 + TB], "xld")

        act.wait_dma("xld")
        for c in range(NC):
            act.op("activation", xsq[:, c * TB:(c + 1) * TB],
                   xb[:, c * TB:(c + 1) * TB], AF.Square)

        pe.after(act).after(dve)
        for c in range(NC):
            pe.op("matmul", ps_sx[:], onc[:], xb[:, c * TB:(c + 1) * TB],
                  start=(c == 0), stop=(c == NC - 1))
        for c in range(NC):
            pe.op("matmul", ps_sq[:], onc[:], xsq[:, c * TB:(c + 1) * TB],
                  start=(c == 0), stop=(c == NC - 1))

        m_r = rows[:, 0 * TB:0 * TB + TB]
        v_r = rows[:, 1 * TB:1 * TB + TB]
        a_r = rows[:, 2 * TB:2 * TB + TB]
        b_r = rows[:, 3 * TB:3 * TB + TB]
        t_r = rows[:, 4 * TB:4 * TB + TB]
        dve.after(pe)
        dve.op("tensor_scalar_mul", m_r, ps_sx[:], 1.0 / D)
        dve.op("tensor_scalar_mul", v_r, ps_sq[:], 1.0 / D)
        dve.op("tensor_mul", t_r, m_r, m_r)
        dve.op("tensor_sub", v_r, v_r, t_r)
        dve.op("tensor_scalar_add", v_r, v_r, cfg["EPS"])
        dve.op("reciprocal", t_r, v_r)
        act.after(dve).op("activation", a_r, t_r, AF.Sqrt)
        dve.after(act)
        dve.op("tensor_mul", t_r, m_r, a_r)
        dve.op("tensor_scalar_mul", b_r, t_r, -1.0)

        pe.after(dve)
        pe.op("matmul", ps_ab[:, 0:TB], onr[:], a_r, start=True, stop=True)
        pe.op("matmul", ps_ab[:, TB:2 * TB], onr[:], b_r, start=True, stop=True)
        dve.after(pe).after(gp).op("tensor_copy", ab_sb[:], ps_ab[:])

        gp.after(dve).wait_dma("st")
        for c in range(NC):
            gp.op("tensor_mul", tmp[:, c * TB:(c + 1) * TB],
                  xb[:, c * TB:(c + 1) * TB], ab_sb[:, 0:TB])
            gp.op("tensor_add", xnb[:, c * TB:(c + 1) * TB],
                  tmp[:, c * TB:(c + 1) * TB], ab_sb[:, TB:2 * TB])
        sp.after(gp)
        for c in range(NC):
            sp.dma(xn[c * 128:(c + 1) * 128, t0:t0 + TB],
                   xnb[:, c * TB:(c + 1) * TB], "st")

        for tt in range(NTT):
            q0 = tt * 128
            pe.after(dve)
            for c in range(NC):
                pe.op("matmul", ps_lg[:], xb[:, c * TB + q0:c * TB + q0 + 128],
                      wrb[:, c * E:(c + 1) * E],
                      start=(c == 0), stop=(c == NC - 1))
            pe.op("matmul", ps_c1[:], a_r[:, q0:q0 + 128], on1[:],
                  start=True, stop=True)
            dve.after(pe)
            dve.op("tensor_copy", acol[:], ps_c1[:])
            dve.op("tensor_copy", brow2[0:1, :], b_r[:, q0:q0 + 128])
            pe.after(dve).op("matmul", ps_c8[:], brow2[:], csbr[:],
                             start=True, stop=True)
            dve.after(pe)
            dve.op("tensor_copy", bc8[:], ps_c8[:])
            lgt = lg[:, 0:E]
            dve.op("tensor_single_scalar", lgt, ps_lg[:], acol[:], ALU.mult)
            dve.op("tensor_add", lgt, lgt, bc8[:])

            if layer == 0:
                mx = lg[:, E:E + 1]
                sm = lg[:, E + 1:E + 2]
                pr = lg[:, 2 * E:3 * E]
                rs = lg[:, E + 2:E + 3]
                dve.op("reduce_max", mx, lgt, axis=AX.X, negate=True)
                act.after(dve).wait_dma("st")
                act.op("activation", pr, lgt, AF.Exp, bias=mx, accum_out=sm)
                dve.after(act)
                dve.op("reciprocal", rs, sm)
                dve.op("tensor_single_scalar", pr, pr, rs, ALU.mult)
                sp.after(dve)
                sp.dma(rout[t0 + q0:t0 + q0 + 128, :], pr, "st")
            else:
                m1 = lg[:, E:E + 1]
                m2 = lg[:, E + 1:E + 2]
                g1w = lg[:, E + 2:E + 3]
                g2w = lg[:, E + 3:E + 4]
                dd = lg[:, E + 4:E + 5]
                eq1 = lg[:, 2 * E:3 * E]
                l2 = lg[:, 3 * E:4 * E]
                eq2 = lg[:, 4 * E:5 * E]
                cmb = lg[:, 5 * E:6 * E]
                dve.wait_dma("st")
                dve.op("reduce_max", m1, lgt, axis=AX.X)
                dve.op("tensor_single_scalar", eq1, lgt, m1, ALU.is_equal)
                dve.op("scalar_tensor_tensor", l2, eq1, NEG_BIG, lgt,
                       ALU.mult, ALU.add)
                dve.op("reduce_max", m2, l2, axis=AX.X)
                dve.op("tensor_single_scalar", eq2, l2, m2, ALU.is_equal)
                dve.op("tensor_sub", dd, m1, m2)
                act.after(dve).op("activation", g1w, dd, AF.Sigmoid)
                dve.after(act)
                dve.op("tensor_scalar", g2w, g1w, -1.0, 1.0, ALU.mult, ALU.add)
                dve.op("tensor_single_scalar", eq1, eq1, g1w, ALU.mult)
                dve.op("tensor_single_scalar", eq2, eq2, g2w, ALU.mult)
                dve.op("tensor_add", cmb, eq1, eq2)
                sp.after(dve)
                sp.dma(rout[t0 + q0:t0 + q0 + 128, :], cmb, "st")

    nc.sync.wait_ge(p.sem("st"), p.val("st"))



def emit_build_caches(p: Prog, cfg, jobs):
    nc = p.nc
    pe, dve, act, gp, sp = p.chains()
    WF = 2048
    wf = p.sbuf("wc_wf", [128, 2 * WF], F32)
    maxcols = max(j["cols"] for j in jobs)
    stg = p.sbuf("wc_stg", [128, maxcols], BF16)
    gp.op("memset", stg[:], 0.0)
    slot = [0]

    def strip(src_ap, ncols, copies, nrows=128):
        s = slot[0]
        slot[0] ^= 1
        sp.after(gp)
        sp.dma(wf[0:nrows, s * WF:s * WF + ncols], src_ap, "wcld")
        gp.wait_dma("wcld").after(dve)
        for so, do, w, neg in copies:
            gp.op("tensor_copy", stg[0:nrows, do:do + w],
                  wf[0:nrows, s * WF + so:s * WF + so + w])
            if neg:
                dve.after(gp)
                for a, b in neg:
                    dve.op("tensor_scalar_mul", stg[0:nrows, do + a:do + b],
                           stg[0:nrows, do + a:do + b], -1.0)

    for j in jobs:
        src_w = j["src"]
        gp.wait_dma("wcst")
        dve.wait_dma("wcst")
        if j["kind"] == "proj":
            Din, Dout = j["Din"], j["Dout"]
            NCI = Din // 128
            for c in range(NCI):
                for o0 in range(0, Dout, WF):
                    ow = min(WF, Dout - o0)
                    copies = []
                    for oc in range(o0 // 128, (o0 + ow) // 128):
                        base = (oc * NCI + c) * 128
                        w = min(128, Dout - oc * 128)
                        neg = []
                        if j.get("negate_swap"):
                            for n0 in (0, 64):
                                nw = max(0, min(32, w - n0))
                                if nw:
                                    neg.append((n0, n0 + nw))
                        copies.append((oc * 128 - o0, base, w, neg))
                    strip(src_w[c * 128:(c + 1) * 128, o0:o0 + ow], ow, copies)
        elif j["kind"] == "up":
            Din, F = j["Din"], j["F"]
            NCI, NCF = Din // 128, (F + 127) // 128
            for c in range(NCI):
                for fh0 in range(0, F, WF):
                    fl = min(WF, F - fh0)
                    copies = []
                    for fc in range(fh0 // 128, (fh0 + fl + 127) // 128):
                        lo_ = max(fc * 128 - fh0, 0)
                        hiw = min((fc + 1) * 128, fh0 + fl) - fh0
                        copies.append(
                            (lo_, (c * NCF + fc) * 128 + (lo_ + fh0 - fc * 128),
                             hiw - lo_, []))
                    strip(src_w[c * 128:(c + 1) * 128, fh0:fh0 + fl], fl, copies)
        elif j["kind"] == "dn":
            F, Dout = j["F"], j["Dout"]
            NCF = (F + 127) // 128
            for fc in range(NCF):
                fw = min(128, F - fc * 128)
                strip(src_w[fc * 128:fc * 128 + fw, :], Dout,
                      [(0, fc * Dout, Dout, [])], nrows=fw)
        elif j["kind"] == "x":
            D, S = j["Din"], j["F"]
            for c in range(D // 128):
                for t0 in range(0, S, WF):
                    tl = min(WF, S - t0)
                    strip(src_w[c * 128:(c + 1) * 128, t0:t0 + tl], tl,
                          [(0, c * S + t0, tl, [])])
        sp.after(gp).after(dve)
        sp.dma(j["dst"], stg[:, 0:j["cols"]], "wcst")
    nc.sync.wait_ge(p.sem("wcst"), p.val("wcst"))



def emit_router_global(p: Prog, cfg, xcache=None):
    nc = p.nc
    pe, dve, act, gp, sp = p.chains()
    S, D, TB, R = cfg["S"], cfg["D"], cfg["TB"], cfg["R"]
    E = cfg["E0"]
    NC = D // 128
    NTB = S // TB
    NTT = TB // 128

    xall = p.tensors["x_all"]
    pall_d = p.tensors["probs_all"]
    wr = p.tensors["w_r"]
    br = p.tensors["b_r"]

    onc = p.sbuf("gr_onc", [128, 1], BF16)
    onr = p.sbuf("gr_onr", [1, 128], F32)
    on1 = p.sbuf("gr_on1", [1, 1], F32)
    xf = p.sbuf("gr_xf", [128, TB], F32)
    xb = p.sbuf("gr_x", [128, NC * TB], BF16)
    xsq = p.sbuf("gr_xsq", [128, NC * TB], BF16)
    wrb = p.sbuf("gr_wr", [128, NC * E], BF16)
    wrf = p.sbuf("gr_wrf", [128, E], F32)
    csbr = p.sbuf("gr_csbr", [2, E], F32)
    brow2 = p.sbuf("gr_brow2", [2, 128], F32)
    rows = p.sbuf("gr_rows", [1, 8 * TB], F32)
    acol = p.sbuf("gr_acol", [128, 1], F32)
    bc8 = p.sbuf("gr_bc8", [128, E], F32)
    lg = p.sbuf("gr_lg", [128, 3 * E + 8], F32)

    ps_sx = p.psum("gr_ps_sx", [1, TB])
    ps_sq = p.psum("gr_ps_sq", [1, TB])
    ps_lg = p.psum("gr_ps_lg", [128, E])
    ps_c1 = p.psum("gr_ps_c1", [128, 1])
    ps_c8 = p.psum("gr_ps_c8", [128, E])
    ps_cs = p.psum("gr_ps_cs", [1, E])

    sp.dma(onc[:], p.tensors["c_ones_col_bf"], "grld")
    sp.dma(onr[:], p.tensors["c_ones_row"], "grld")
    sp.dma(on1[:], p.tensors["c_one1"], "grld")
    sp.dma(brow2[1:2, :], p.tensors["c_ones_row"], "grld")
    sp.dma(csbr[1:2, :], br, "grld")
    gp.wait_dma("grld")
    gp.op("engine_nop")
    for c in range(NC):
        sp.after(gp)
        sp.dma(wrf[:], wr[c * 128:(c + 1) * 128, :], "grld")
        gp.wait_dma("grld")
        gp.op("tensor_copy", wrb[:, c * E:(c + 1) * E], wrf[:])
    pe.after(gp)
    for c in range(NC):
        last = pe.op("matmul", ps_cs[:], onc[:], wrb[:, c * E:(c + 1) * E],
                     start=(c == 0), stop=(c == NC - 1))
    dve.after(pe)
    dve.op("tensor_copy", csbr[0:1, :], ps_cs[:])

    for b in range(R):
        for tb in range(NTB):
            t0 = tb * TB
            if xcache is not None:
                sp.after(pe).after(gp)
                sp.dma(xb[:].rearrange("p (c t) -> p c t", t=TB),
                       xcache[b].rearrange("p (c s) -> p c s", s=S)[:, :,
                                                                   t0:t0 + TB],
                       "grxld")
                gp.wait_dma("grxld").after(pe)
                for c in range(NC):
                    gp.op("tensor_mul", xsq[:, c * TB:(c + 1) * TB],
                          xb[:, c * TB:(c + 1) * TB], xb[:, c * TB:(c + 1) * TB])
            else:
                for c in range(NC):
                    sp.after(gp)
                    sp.dma(xf[:], xall[b, c * 128:(c + 1) * 128, t0:t0 + TB],
                           "grxld")
                    gp.wait_dma("grxld").after(pe)
                    gp.op("tensor_copy", xb[:, c * TB:(c + 1) * TB], xf[:])
                    gp.op("tensor_mul", xsq[:, c * TB:(c + 1) * TB],
                          xb[:, c * TB:(c + 1) * TB], xb[:, c * TB:(c + 1) * TB])
            pe.after(gp).after(dve)
            for c in range(NC):
                pe.op("matmul", ps_sx[:], onc[:], xb[:, c * TB:(c + 1) * TB],
                      start=(c == 0), stop=(c == NC - 1))
            for c in range(NC):
                pe.op("matmul", ps_sq[:], onc[:], xsq[:, c * TB:(c + 1) * TB],
                      start=(c == 0), stop=(c == NC - 1))
            m_r = rows[:, 0:TB]
            v_r = rows[:, TB:2 * TB]
            a_r = rows[:, 2 * TB:3 * TB]
            b_r = rows[:, 3 * TB:4 * TB]
            t_r = rows[:, 4 * TB:5 * TB]
            dve.after(pe)
            dve.op("tensor_scalar_mul", m_r, ps_sx[:], 1.0 / D)
            dve.op("tensor_scalar_mul", v_r, ps_sq[:], 1.0 / D)
            dve.op("tensor_mul", t_r, m_r, m_r)
            dve.op("tensor_sub", v_r, v_r, t_r)
            dve.op("tensor_scalar_add", v_r, v_r, cfg["EPS"])
            dve.op("reciprocal", t_r, v_r)
            act.after(dve)
            act.op("activation", a_r, t_r, AF.Sqrt)
            dve.after(act)
            dve.op("tensor_mul", t_r, m_r, a_r)
            dve.op("tensor_scalar_mul", b_r, t_r, -1.0)
            for tt in range(NTT):
                q0 = tt * 128
                pe.after(dve)
                for c in range(NC):
                    pe.op("matmul", ps_lg[:],
                          xb[:, c * TB + q0:c * TB + q0 + 128],
                          wrb[:, c * E:(c + 1) * E],
                          start=(c == 0), stop=(c == NC - 1))
                pe.op("matmul", ps_c1[:], a_r[:, q0:q0 + 128], on1[:],
                      start=True, stop=True)
                dve.after(pe)
                dve.op("tensor_copy", acol[:], ps_c1[:])
                dve.op("tensor_copy", brow2[0:1, :], b_r[:, q0:q0 + 128])
                pe.after(dve)
                pe.op("matmul", ps_c8[:], brow2[:], csbr[:],
                      start=True, stop=True)
                dve.after(pe)
                dve.op("tensor_copy", bc8[:], ps_c8[:])
                lgt = lg[:, 0:E]
                dve.op("tensor_single_scalar", lgt, ps_lg[:], acol[:], ALU.mult)
                dve.op("tensor_add", lgt, lgt, bc8[:])
                mx = lg[:, E:E + 1]
                sm = lg[:, E + 1:E + 2]
                rs = lg[:, E + 2:E + 3]
                pr = lg[:, 2 * E:3 * E]
                dve.op("reduce_max", mx, lgt, axis=AX.X, negate=True)
                act.after(dve).wait_dma("grst")
                act.op("activation", pr, lgt, AF.Exp, bias=mx, accum_out=sm)
                dve.after(act)
                dve.op("reciprocal", rs, sm)
                dve.op("tensor_single_scalar", pr, pr, rs, ALU.mult)
                sp.after(dve)
                sp.dma(pall_d[b * S + t0 + q0:b * S + t0 + q0 + 128, :], pr,
                       "grst")
    nc.sync.wait_ge(p.sem("grst"), p.val("grst"))



def alloc_proj_bufs(p: Prog, cfg, max_nci, max_ncio, rope, norms):
    TB = cfg["TB"]
    bufs = {
        "wf": p.sbuf("pj_wf", [128, max_nci * 128], F32),
        "wb": p.sbuf("pj_wb", [128, max_ncio * 128], BF16),
        "xbuf": p.sbuf("pj_x", [128, max_nci * TB], BF16),
        "ev": p.sbuf("pj_ev", [128, TB], BF16),
        "ps": p.psum("pj_ps", [128, TB]),
    }
    if rope:
        bufs["w2b"] = p.sbuf("pj_w2b", [128, max_ncio * 128], BF16)
        bufs["ev2"] = p.sbuf("pj_ev2", [128, TB], BF16)
        bufs["evr"] = p.sbuf("pj_evr", [128, 2 * TB], BF16)
        bufs["ps2"] = p.psum("pj_ps2", [128, TB])
    if norms:
        bufs["sq"] = p.sbuf("pj_sq", [128, TB], F32)
        bufs["psn"] = p.psum("pj_psn", [1, TB])
    return bufs


def emit_projection(p: Prog, cfg, tag, bufs, src_name, w_name, dst_name,
                    Din, Dout, mode="fm", rope=False, norms=None, w2_name=None,
                    wc=None):
    nc = p.nc
    pe, dve, act, gp, sp = p.chains()
    S, TB = cfg["S"], cfg["TB"]
    NCI = Din // 128
    NCO = (Dout + 127) // 128
    NTB = S // TB
    src = p.tensors[src_name]
    dst = p.tensors[dst_name]
    w = p.tensors[w_name]

    wf, wb, xbuf, ev, ps = (bufs["wf"], bufs["wb"], bufs["xbuf"], bufs["ev"],
                            bufs["ps"])
    w2b = bufs.get("w2b")
    ev2 = bufs.get("ev2")
    evr = bufs.get("evr")
    sq = bufs.get("sq")
    ps2 = bufs.get("ps2")
    psn = bufs.get("psn")

    if wc is not None:
        sp.after(pe)
        sp.dma(wb[:, 0:NCI * NCO * 128], wc[0], "pwld")
        if rope:
            sp.dma(w2b[:, 0:NCI * NCO * 128], wc[1], "pwld")
        pe.wait_dma("pwld")
        wsrcs = ()
    else:
        wsrcs = ((w, wb),) + (((p.tensors[w2_name], w2b),) if rope else ())
    for ws, wdst in wsrcs:
        for oc in range(NCO):
            o0 = oc * 128
            ow = min(128, Dout - o0)
            for c in range(NCI):
                sp.after(gp)
                sp.dma(wf[:, c * 128:c * 128 + ow],
                       ws[c * 128:(c + 1) * 128, o0:o0 + ow], "pwld")
            gp.wait_dma("pwld").after(pe)
            for c in range(NCI):
                gp.op("tensor_copy",
                      wdst[:, (oc * NCI + c) * 128:(oc * NCI + c) * 128 + ow],
                      wf[:, c * 128:c * 128 + ow])
            if ws is not w:
                dve.after(gp)
                for c in range(NCI):
                    base = (oc * NCI + c) * 128
                    for neg0 in (0, 64):
                        nw = max(0, min(32, ow - neg0))
                        if nw:
                            dve.op("tensor_scalar_mul",
                                   wdst[:, base + neg0:base + neg0 + nw],
                                   wdst[:, base + neg0:base + neg0 + nw], -1.0)
                gp.after(dve)
                gp.op("engine_nop")

    for tb in range(NTB):
        t0 = tb * TB
        sp.after(pe).after(gp).after(act)
        for c in range(NCI):
            sp.dma(xbuf[:, c * TB:(c + 1) * TB],
                   src[c * 128:(c + 1) * 128, t0:t0 + TB], "pxld")
        act.wait_dma("pst")
        gp.wait_dma("pst")
        pe.wait_dma("pxld").after(gp)
        for oc in range(NCO):
            o0 = oc * 128
            ow = min(128, Dout - o0)
            if mode == "fm":
                pe.after(act).after(dve).after(gp)
                for c in range(NCI):
                    pe.op("matmul", ps[0:ow, :],
                          wb[:, (oc * NCI + c) * 128:(oc * NCI + c) * 128 + ow],
                          xbuf[:, c * TB:(c + 1) * TB],
                          start=(c == 0), stop=(c == NCI - 1))
                if rope:
                    for c in range(NCI):
                        pe.op("matmul", ps2[0:ow, :],
                              w2b[:, (oc * NCI + c) * 128:(oc * NCI + c) * 128 + ow],
                              xbuf[:, c * TB:(c + 1) * TB],
                              start=(c == 0), stop=(c == NCI - 1))
                act.after(pe).wait_dma("pst")
                act.op("activation", ev[0:ow, :], ps[0:ow, :], AF.Copy)
                if norms:
                    act.op("activation", sq[0:ow, :], ps[0:ow, :], AF.Square)
                    H_per = ow // cfg["HD"]
                    for hh in range(H_per):
                        h = (o0 // cfg["HD"]) + hh
                        pb = hh * cfg["HD"]
                        pe.after(act).after(dve)
                        pe.op("matmul", psn[:],
                              p.tensors["sb_onc"][pb:pb + cfg["HD"], 0:1],
                              sq[pb:pb + cfg["HD"], :], start=True, stop=True)
                        dve.after(pe)
                        nrow = norms["row"]
                        dve.op("reduce_max",
                               nrow[0:1, h * NTB + tb:h * NTB + tb + 1],
                               psn[:], axis=AX.X)
                if rope:
                    act.op("activation", ev2[0:ow, :], ps2[0:ow, :], AF.Copy)
                    cosb = p.tensors["sb_cos2"]
                    sinb = p.tensors["sb_sin2"]
                    gp.after(act)
                    gp.op("tensor_mul", evr[:, 0:TB][0:ow, :], ev[0:ow, :],
                          cosb[0:ow, t0:t0 + TB])
                    gp.op("tensor_mul", evr[:, TB:2 * TB][0:ow, :], ev2[0:ow, :],
                          sinb[0:ow, t0:t0 + TB])
                    gp.op("tensor_add", ev[0:ow, :], evr[:, 0:TB][0:ow, :],
                          evr[:, TB:2 * TB][0:ow, :])
                    sp.after(gp)
                else:
                    sp.after(act)
                sp.dma(dst[o0:o0 + ow, t0:t0 + TB], ev[0:ow, :], "pst")
            else:
                for ttile in range(TB // 128):
                    pe.after(act)
                    for c in range(NCI):
                        pe.op("matmul", ps[:, 0:ow],
                              xbuf[:, c * TB + ttile * 128:c * TB + ttile * 128 + 128],
                              wb[:, (oc * NCI + c) * 128:(oc * NCI + c) * 128 + ow],
                              start=(c == 0), stop=(c == NCI - 1))
                    act.after(pe).wait_dma("pst")
                    act.op("activation", ev[:, 0:ow], ps[:, 0:ow], AF.Copy)
                    sp.after(act)
                    sp.dma(dst[t0 + ttile * 128:t0 + ttile * 128 + 128, o0:o0 + ow],
                           ev[:, 0:ow], "pst")
    return p.val("pst")


def emit_attn(p: Prog, cfg, layer: int, caches=None):
    nc = p.nc
    pe, dve, act, gp, sp = p.chains()
    S, D, H, HD, TB = cfg["S"], cfg["D"], cfg["H"], cfg["HD"], cfg["TB"]
    NC = D // 128
    NTB = S // TB
    NKT = S // 128
    QD = H * HD
    scale0 = 1.0 / math.sqrt(D)
    scale1 = 1.0 / math.sqrt(HD)

    xn = p.tensors["xn_fm"]
    a_out = p.tensors["attn_fm"]

    onc = p.sbuf("sb_onc", [128, 1], F32)
    oncb = p.sbuf("sb_oncb", [128, 1], BF16)
    onr = p.sbuf("sb_onr", [1, 128], F32)
    sp.dma(onc[:], p.tensors["c_ones_col"], "ld")
    sp.dma(oncb[:], p.tensors["c_ones_col_bf"], "ld")
    sp.dma(onr[:], p.tensors["c_ones_row"], "ld")

    nrow_q = p.sbuf("at_nq", [1, H * NTB], F32)
    nrow_k = p.sbuf("at_nk", [1, H * NTB], F32)
    negc = p.sbuf("at_negc", [1, H * NTB + 2 * H], F32)

    if layer == 1:
        cos2 = p.sbuf("sb_cos2", [128, S], BF16)
        sin2 = p.sbuf("sb_sin2", [128, S], BF16)
        sp.dma(cos2[:], p.tensors["c_cos2"], "ld")
        sp.dma(sin2[:], p.tensors["c_sin2"], "ld")
        gp.wait_dma("ld")
    else:
        gb = p.sbuf("at_gb", [128, 2 * (D // 128)], F32)
        for c in range(D // 128):
            sp.dma(gb[:, c:c + 1],
                   p.tensors["ln_gamma"][c * 128:(c + 1) * 128, :], "ld")
            sp.dma(gb[:, D // 128 + c:D // 128 + c + 1],
                   p.tensors["ln_beta"][c * 128:(c + 1) * 128, :], "ld")

    pe.wait_dma("ld")
    dve.wait_dma("ld")
    act.wait_dma("ld")
    gp.wait_dma("ld")

    q_fm = p.dtmp("at_q", [QD, S], BF16)
    k_fm = p.dtmp("at_k", [QD, S], BF16)
    v_tm = p.dtmp("at_v", [S, QD], BF16)
    cz = caches or {}
    if layer == 0:
        bufs = alloc_proj_bufs(p, cfg, D // 128, (D // 128) * (QD // 128),
                               rope=False, norms=True)
        st1 = emit_projection(p, cfg, "pq", bufs, "xn_fm", "w_q", "at_q", D, QD,
                              "fm", norms={"row": nrow_q}, wc=cz.get("q"))
        st2 = emit_projection(p, cfg, "pk", bufs, "xn_fm", "w_k", "at_k", D, QD,
                              "fm", norms={"row": nrow_k}, wc=cz.get("k"))
        st3 = emit_projection(p, cfg, "pv", bufs, "xn_fm", "w_v", "at_v", D, QD,
                              "tm", wc=cz.get("v"))
    else:
        LAT = cfg["LAT"]
        bufs = alloc_proj_bufs(p, cfg, D // 128, (D // 128) * (QD // 128),
                               rope=True, norms=True)
        kv_fm = p.dtmp("at_kv", [LAT, S], BF16)
        stkv = emit_projection(p, cfg, "pkv", bufs, "xn_fm", "w_kvd", "at_kv",
                               D, LAT, "fm", wc=cz.get("kvd"))
        st1 = emit_projection(p, cfg, "pq", bufs, "xn_fm", "w_q", "at_q", D, QD,
                              "fm", rope=True, norms={"row": nrow_q},
                              w2_name="w_q2", wc=cz.get("q"))
        pe.wait_dma("pst", stkv)
        sp.wait_dma("pst", stkv)
        st2 = emit_projection(p, cfg, "pk", bufs, "at_kv", "w_ku", "at_k", LAT,
                              QD, "fm", rope=True, norms={"row": nrow_k},
                              w2_name="w_ku2", wc=cz.get("ku"))
        st3 = emit_projection(p, cfg, "pv", bufs, "at_kv", "w_vu", "at_v", LAT,
                              QD, "tm", wc=cz.get("vu"))

    scale = scale0 if layer == 0 else scale1
    nkm = negc[0:1, H * NTB:H * NTB + H]
    dve.op("reduce_max", nkm.rearrange("o (h x) -> o h x", h=H),
           nrow_k[0:1, :].rearrange("o (h t) -> o h t", h=H), axis=AX.X)
    dve.op("tensor_tensor",
           negc[0:1, 0:H * NTB].rearrange("o (h t) -> o h t", h=H),
           nrow_q[0:1, :].rearrange("o (h t) -> o h t", h=H),
           nkm.unsqueeze(2).broadcast_to([1, H, NTB]), ALU.mult)
    act.after(dve)
    act.op("activation", negc[0:1, 0:H * NTB], negc[0:1, 0:H * NTB], AF.Sqrt)
    dve.after(act)
    dve.op("tensor_scalar_mul", negc[0:1, 0:H * NTB], negc[0:1, 0:H * NTB],
           -scale)

    vall = p.sbuf("at_vall", [128, NKT * QD], BF16)
    kbuf = p.sbuf("at_kb", [64, 2 * S], BF16)
    qbuf = p.sbuf("at_qb", [64, 2 * TB], BF16)
    vh = p.sbuf("at_vh", [128, 2 * (HD + 1)], BF16)
    ptb = p.sbuf("at_pt", [128, 2 * TB], BF16)
    sd = p.sbuf("at_sd", [128, 2 * TB], F32)
    drow = p.sbuf("at_dr", [1, TB], F32)
    dbc = p.sbuf("at_dbc", [64, TB], F32)
    ccall = p.sbuf("at_ccall", [128, H * NTB], F32)
    ocat = p.sbuf("at_ocat", [128, NC * TB], BF16)
    dist = p.sbuf("at_dist", [128, NKT * TB], BF16) if layer == 0 else None
    NDIAG = TB // 128 + 1
    cmsk = p.sbuf("at_cmsk", [128, NDIAG * TB], F32) if layer == 1 else None

    ps_s = bufs["ps"]
    ps_s2 = p.psum("at_ps_s2", [128, TB])
    ps_o = p.psum("at_ps_o", [HD + 1, TB])
    ps_bc = p.psum("at_ps_bc", [128, TB])
    ps_b = ps_bc[0:64, :]

    gp.op("memset", vh[:, HD:HD + 1], 1.0)
    gp.op("memset", vh[:, (HD + 1) + HD:(HD + 1) + HD + 1], 1.0)

    sp.wait_dma("pst", st3)
    pe.wait_dma("pst", st3)

    pe.after(dve)
    pe.op("matmul", ps_bc[:, 0:H * NTB], onr[:], negc[0:1, 0:H * NTB],
          start=True, stop=True)
    dve.after(pe)
    dve.op("tensor_copy", ccall[:], ps_bc[:, 0:H * NTB])
    act.after(dve)

    if layer == 0:
        rows = p.sbuf("at_rows", [1, 8 * TB], F32)
        absb = p.sbuf("at_absb", [128, 2 * TB], F32)
        aout_t = p.sbuf("at_aot", [128, TB], BF16)
        sqc = p.sbuf("at_sqc", [128, TB], F32)
        ps_sx = bufs["psn"]
        ps_sq = p.psum("at_ps_sq", [1, TB])
        ps_a = p.psum("at_ps_a", [128, TB])
        ps_b2 = p.psum("at_ps_b2", [128, TB])
    else:
        woT = p.sbuf("at_wo", [128, NC * QD], BF16)
        if cz.get("wo") is not None:
            sp.after(pe)
            sp.dma(woT[:], cz["wo"][0], "ldw")
            pe.wait_dma("ldw")
        else:
            wof = bufs["wf"]
            wo = p.tensors["w_o"]
            for c in range(NC):
                sp.after(gp)
                sp.dma(wof[:, 0:QD], wo[c * 128:(c + 1) * 128, :], "ldw")
                gp.wait_dma("ldw").after(pe)
                gp.op("tensor_copy", woT[:, c * QD:(c + 1) * QD], wof[:, 0:QD])
        yps = bufs["ps2"]
        yev = p.sbuf("at_yev", [128, TB], BF16)

    sp.after(gp)
    sp.dma(vall[:].rearrange("p (kt d) -> p kt d", d=QD),
           v_tm.rearrange("(kt p) d -> p kt d", p=128), "vld")
    gp.wait_dma("vld")
    gp.op("engine_nop")

    def nkt_of(qb_):
        q0_ = qb_ * TB
        return NKT if layer == 0 else min((q0_ + TB + 127) // 128, NKT)

    iters = [(qb, h) for qb in range(NTB) for h in range(H)]
    qkld_tot = {}

    def loads_for(i, guard_tick):
        if i >= len(iters):
            return
        qb_, h_ = iters[i]
        sl = i % 2
        sem = "qkld%d" % sl
        q0_ = qb_ * TB
        nk = nkt_of(qb_)
        if guard_tick:
            sp.after("pe_tick", guard_tick)
        sp.dma(qbuf[:, sl * TB:sl * TB + TB],
               q_fm[h_ * HD:h_ * HD + HD, q0_:q0_ + TB], sem)
        sp.dma(kbuf[:, sl * S:sl * S + nk * 128],
               k_fm[h_ * HD:h_ * HD + HD, 0:nk * 128], sem)
        qkld_tot[i] = p.val(sem)

    loads_for(0, 0)
    bank_cons = [None, None]
    last_exp = [0, 0]
    last_pv = [0, 0]
    o_cons = 0
    dist_qb = -1
    diag_map = {}

    for i, (qb, h) in enumerate(iters):
        q0 = qb * TB
        sl = i % 2
        idx = h * NTB + qb
        nkt_here = nkt_of(qb)
        if h == 0 and qb != dist_qb:
            if layer == 0:
                gp.after(dve)
                for kt in range(NKT):
                    gp.op("iota", dist[:, kt * TB:(kt + 1) * TB], [[-1, TB]],
                          base=kt * 128 - q0, channel_multiplier=1,
                          allow_small_or_imprecise_dtypes=True)
                dve.after(gp)
                for kt in range(NKT):
                    dsl = dist[:, kt * TB:(kt + 1) * TB].bitcast(mybir.dt.uint16)
                    dve.op("tensor_single_scalar", dsl, dsl, 0x7FFF,
                           ALU.bitwise_and)
            else:
                diag_map = {}
                gp.after(dve)
                for kt in range(nkt_here):
                    if kt * 128 + 127 >= q0:
                        j = len(diag_map)
                        diag_map[kt] = j
                        gp.op("iota", cmsk[:, j * TB:(j + 1) * TB], [[1, TB]],
                              base=q0 - kt * 128, channel_multiplier=-1,
                              allow_small_or_imprecise_dtypes=True)
                dve.after(gp)
                for kt, j in diag_map.items():
                    dve.op("tensor_scalar", cmsk[:, j * TB:(j + 1) * TB],
                           cmsk[:, j * TB:(j + 1) * TB], 0.0, 1.0e28,
                           ALU.min, ALU.mult)
            dist_qb = qb
        pe.wait_dma("qkld%d" % sl, qkld_tot[i])
        loads_for(i + 1, pe.t)
        s_first = 0
        exp_tick = {}
        gpv_tick = {}
        pv_tick = {}
        for step in range(nkt_here + 1):
            if step < nkt_here:
                kt = step
                bank = kt % 2
                psb = ps_s if bank == 0 else ps_s2
                if bank_cons[bank]:
                    pe.after(*bank_cons[bank])
                pe.op("matmul", psb[:],
                      kbuf[:, sl * S + kt * 128:sl * S + kt * 128 + 128][0:HD, :],
                      qbuf[:, sl * TB:sl * TB + TB][0:HD, :],
                      start=True, stop=True)
                if s_first == 0:
                    s_first = pe.t
                s_t = pe.t
                sdsl = sd[:, bank * TB:(bank + 1) * TB]
                ptsl = ptb[:, bank * TB:(bank + 1) * TB]
                if layer == 0:
                    dve.after("pe_tick", s_t)
                    if last_exp[bank]:
                        dve.after("act_tick", last_exp[bank])
                    dve.op("tensor_mul", sdsl, psb[:],
                           dist[:, kt * TB:(kt + 1) * TB])
                    bank_cons[bank] = ("dve_tick", dve.t)
                    act.after("dve_tick", dve.t)
                    if last_pv[bank]:
                        act.after("pe_tick", last_pv[bank])
                    act.op("activation", ptsl, sdsl, AF.Exp,
                           bias=ccall[:, idx:idx + 1], scale=scale0 / S)
                elif kt in diag_map:
                    dve.after("pe_tick", s_t)
                    if last_exp[bank]:
                        dve.after("act_tick", last_exp[bank])
                    dve.op("tensor_add", sdsl, psb[:],
                           cmsk[:, diag_map[kt] * TB:(diag_map[kt] + 1) * TB])
                    bank_cons[bank] = ("dve_tick", dve.t)
                    act.after("dve_tick", dve.t)
                    if last_pv[bank]:
                        act.after("pe_tick", last_pv[bank])
                    act.op("activation", ptsl, sdsl, AF.Exp,
                           bias=ccall[:, idx:idx + 1], scale=scale1)
                else:
                    act.after("pe_tick", s_t)
                    if last_pv[bank]:
                        act.after("pe_tick", last_pv[bank])
                    act.op("activation", ptsl, psb[:], AF.Exp,
                           bias=ccall[:, idx:idx + 1], scale=scale1)
                    bank_cons[bank] = ("act_tick", act.t)
                exp_tick[kt] = act.t
                last_exp[bank] = act.t
                if last_pv[bank]:
                    gp.after("pe_tick", last_pv[bank])
                gp.op("tensor_copy",
                      vh[:, bank * (HD + 1):bank * (HD + 1) + HD],
                      vall[:, kt * QD + h * HD:kt * QD + h * HD + HD])
                gpv_tick[kt] = gp.t
            if step >= 1:
                kt = step - 1
                bank = kt % 2
                pe.after("act_tick", exp_tick[kt])
                pe.after("gp_tick", gpv_tick[kt])
                if kt == 0 and o_cons:
                    pe.after("dve_tick", o_cons)
                pe.op("matmul", ps_o[:],
                      vh[:, bank * (HD + 1):bank * (HD + 1) + HD + 1],
                      ptb[:, bank * TB:(bank + 1) * TB],
                      start=(kt == 0), stop=(kt == nkt_here - 1))
                pv_tick[kt] = pe.t
                last_pv[bank] = pe.t
        dve.after(pe)
        dve.op("reciprocal", drow[:], ps_o[HD:HD + 1, :])
        pe.after(dve)
        pe.op("matmul", ps_b, onr[0:1, 0:64], drow[:], start=True, stop=True)
        dve.after(pe)
        dve.op("tensor_copy", dbc[:], ps_b)
        oc_, half = h // 2, h % 2
        dve.op("tensor_mul",
               ocat[:, oc_ * TB:(oc_ + 1) * TB][half * 64:half * 64 + 64, :],
               ps_o[0:HD, :], dbc[:])
        o_cons = dve.t

        if h != H - 1:
            continue
        if layer == 0:
            act.after(dve)
            for c in range(NC):
                if c > 0:
                    act.after(pe)
                act.op("activation", sqc[:], ocat[:, c * TB:(c + 1) * TB],
                       AF.Square)
                pe.after(act).after(dve)
                pe.op("matmul", ps_sx[:], oncb[:], ocat[:, c * TB:(c + 1) * TB],
                      start=(c == 0), stop=(c == NC - 1))
                pe.op("matmul", ps_sq[:], onc[:], sqc[:],
                      start=(c == 0), stop=(c == NC - 1))
            m_r = rows[:, 0:TB]
            v_r = rows[:, TB:2 * TB]
            a_r = rows[:, 2 * TB:3 * TB]
            b_r = rows[:, 3 * TB:4 * TB]
            t_r = rows[:, 4 * TB:5 * TB]
            dve.after(pe)
            dve.op("tensor_scalar_mul", m_r, ps_sx[:], 1.0 / D)
            dve.op("tensor_scalar_mul", v_r, ps_sq[:], 1.0 / D)
            dve.op("tensor_mul", t_r, m_r, m_r)
            dve.op("tensor_sub", v_r, v_r, t_r)
            dve.op("tensor_scalar_add", v_r, v_r, cfg["EPS"])
            dve.op("reciprocal", t_r, v_r)
            act.after(dve).op("activation", a_r, t_r, AF.Sqrt)
            dve.after(act)
            dve.op("tensor_mul", t_r, m_r, a_r)
            dve.op("tensor_scalar_mul", b_r, t_r, -1.0)
            pe.after(dve)
            pe.op("matmul", ps_a[:], onr[:], a_r, start=True, stop=True)
            pe.op("matmul", ps_b2[:], onr[:], b_r, start=True, stop=True)
            dve.after(pe)
            dve.op("tensor_copy", absb[:, 0:TB], ps_a[:])
            dve.op("tensor_copy", absb[:, TB:2 * TB], ps_b2[:])
            for c in range(NC):
                dve.op("tensor_mul", sqc[:], ocat[:, c * TB:(c + 1) * TB],
                       absb[:, 0:TB])
                dve.op("tensor_add", sqc[:], sqc[:], absb[:, TB:2 * TB])
                dve.wait_dma("ast")
                dve.op("tensor_scalar", aout_t[:], sqc[:], gb[:, c:c + 1],
                       gb[:, NC + c:NC + c + 1], ALU.mult, ALU.add)
                sp.after(dve)
                sp.dma(a_out[c * 128:(c + 1) * 128, q0:q0 + TB], aout_t[:],
                       "ast")
            o_cons = dve.t
        else:
            for oc2 in range(NC):
                pe.after(dve).after(act)
                for c in range(NC):
                    pe.op("matmul", yps[:],
                          woT[:, c * QD + oc2 * 128:c * QD + oc2 * 128 + 128],
                          ocat[:, c * TB:(c + 1) * TB],
                          start=(c == 0), stop=(c == NC - 1))
                act.after(pe).wait_dma("ast")
                act.op("activation", yev[:], yps[:], AF.Copy)
                sp.after(act)
                sp.dma(a_out[oc2 * 128:(oc2 + 1) * 128, q0:q0 + TB], yev[:],
                       "ast")
            dve.after(act)
            dve.op("engine_nop")
            o_cons = dve.t

    nc.sync.wait_ge(p.sem("ast"), p.val("ast"))


def build_attn(cfg, layer: int):
    p = Prog()
    S, D, H, HD = cfg["S"], cfg["D"], cfg["H"], cfg["HD"]
    QD = H * HD
    declare_consts(p)
    p.din("xn_fm", [D, S], BF16)
    if layer == 0:
        p.din("w_q", [D, QD], F32)
        p.din("w_k", [D, QD], F32)
        p.din("w_v", [D, QD], F32)
        p.din("ln_gamma", [D, 1], F32)
        p.din("ln_beta", [D, 1], F32)
    else:
        LAT = cfg["LAT"]
        p.din("w_q", [D, QD], F32)
        p.din("w_q2", [D, QD], F32)
        p.din("w_kvd", [D, LAT], F32)
        p.din("w_ku", [LAT, QD], F32)
        p.din("w_ku2", [LAT, QD], F32)
        p.din("w_vu", [LAT, QD], F32)
        p.din("w_o", [QD, D], F32)
        p.din("c_cos2", [128, S], BF16)
        p.din("c_sin2", [128, S], BF16)
    p.dout("attn_fm", [D, S], BF16)
    emit_attn(p, cfg, layer)
    p.finish()
    return p



def emit_ffn(p: Prog, cfg, layer: int, caches=None):
    nc = p.nc
    pe, dve, act, gp, sp = p.chains()
    S, D = cfg["S"], cfg["D"]
    FTB = cfg["FTB"]
    NC = D // 128
    NFT = S // FTB
    if layer == 0:
        E, F = cfg["E0"], cfg["F0"]
        ET = E
        afunc = AF.Silu
    else:
        E, F = cfg["E1"], cfg["HID1"]
        ET = E + 1
        afunc = AF.Gelu
    NCF = (F + 127) // 128
    NTOK = cfg["NTOK"]
    RT = NTOK // 128
    KSEL = NTOK // cfg["E0"]

    xn = p.tensors["xn_fm"]
    f_out = p.tensors["ffn_fm"]
    rout = p.tensors["probs" if layer == 0 else "combine"]

    onc = p.sbuf("sb_onc", [128, 1], F32)
    onr = p.sbuf("sb_onr", [1, 128], F32)
    idn = p.sbuf("sb_idn", [128, 128], F32)
    sp.dma(onc[:], p.tensors["c_ones_col"], "ld")
    sp.dma(onr[:], p.tensors["c_ones_row"], "ld")
    sp.dma(idn[:], p.tensors["c_iden"], "ld")

    gmT = p.sbuf("ff_gmT", [ET, S], F32)
    bdall = p.sbuf("ff_bd", [ET, D], F32)
    rtile = p.sbuf("ff_rt", [128, 2 * E], F32)
    thr_b = p.sbuf("ff_thrb", [128, E], F32)
    ev = p.sbuf("ff_ev", [128, FTB], BF16)

    ALN = max(FTB, 512)
    ps = p.psum("ff_ps", [128, max(NC, 3) * ALN])
    ps0 = ps[:, 0:FTB]

    if layer == 0:
        sp.dma(bdall[:], p.tensors["b_down"], "ld")
    else:
        sp.dma(bdall[0:E, :], p.tensors["b2"], "ld")
        sp.dma(bdall[E:E + 1, :], p.tensors["bs2"], "ld")
    pe.wait_dma("ld")
    dve.wait_dma("ld")
    gp.wait_dma("ld")

    if layer == 0:
        pall = p.sbuf("ff_pall", [128, RT * E], F32)
        msk = p.sbuf("ff_msk", [128, RT * E], F32)
        cnt = p.sbuf("ff_cnt", [128, E], F32)
        rows = p.sbuf("ff_rows", [1, 8 * E], F32)
        sp.dma(pall[:].rearrange("p (r e) -> p r e", e=E),
               p.tensors["probs_all"].rearrange("(r p) e -> p r e", p=128),
               "pald")
        lo = rows[:, 0:E]
        hi = rows[:, E:2 * E]
        md = rows[:, 2 * E:3 * E]
        ge = rows[:, 3 * E:4 * E]
        t1 = rows[:, 4 * E:5 * E]
        cr = rows[:, 5 * E:6 * E]
        dve.wait_dma("pald")
        dve.op("memset", lo, 0.0)
        dve.op("memset", hi, 1.0)
        pv = pall[:].rearrange("p (r e) -> p r e", e=E)
        mv = msk[:].rearrange("p (r e) -> p r e", e=E)
        mrv = msk[:].rearrange("p (r e) -> p e r", e=E)
        for it in range(30):
            dve.op("tensor_add", md, lo, hi)
            dve.op("tensor_scalar_mul", md, md, 0.5)
            pe.after(dve)
            pe.op("matmul", ps0[:, 0:E], onr[:], md, start=True, stop=True)
            dve.after(pe)
            dve.op("tensor_copy", thr_b[:], ps0[:, 0:E])
            dve.op("tensor_tensor", mv, pv,
                   thr_b[:].unsqueeze(1).broadcast_to([128, RT, E]), ALU.is_ge)
            dve.op("reduce_sum", cnt[:], mrv, axis=AX.X)
            pe.after(dve)
            pe.op("matmul", ps0[0:1, 0:E], onc[:], cnt[:],
                  start=True, stop=True)
            dve.after(pe)
            dve.op("tensor_copy", cr, ps0[0:1, 0:E])
            dve.op("tensor_single_scalar", ge, cr, float(KSEL), ALU.is_ge)
            dve.op("tensor_sub", t1, md, lo)
            dve.op("tensor_mul", t1, ge, t1)
            dve.op("tensor_add", lo, lo, t1)
            dve.op("tensor_sub", t1, hi, md)
            dve.op("tensor_mul", t1, ge, t1)
            dve.op("tensor_add", hi, md, t1)
        pe.after(dve)
        pe.op("matmul", ps0[:, 0:E], onr[:], lo, start=True, stop=True)
        dve.after(pe)
        dve.op("tensor_copy", thr_b[:], ps0[:, 0:E])

    for tt in range(S // 128):
        sp.after(pe).after(dve)
        sp.dma(rtile[:, 0:E], rout[tt * 128:(tt + 1) * 128, :], "rld")
        dve.wait_dma("rld")
        gmt = rtile[:, E:2 * E]
        if layer == 0:
            dve.op("tensor_tensor", gmt, rtile[:, 0:E], thr_b[:], ALU.is_ge)
            dve.op("tensor_mul", gmt, gmt, rtile[:, 0:E])
        else:
            dve.op("tensor_copy", gmt, rtile[:, 0:E])
        pe.after(dve)
        pe.op("transpose", ps0[0:E, 0:128], gmt, idn[:])
        dve.after(pe)
        dve.op("tensor_copy", gmT[0:E, tt * 128:(tt + 1) * 128],
               ps0[0:E, 0:128])
    if layer == 1:
        sp.dma(gmT[E:E + 1, :], p.tensors["c_ones_srow"], "orow")
        dve.wait_dma("orow")
        dve.op("engine_nop")
    gm_dram = p.dtmp("gm_dram", [ET, S], F32)
    sp.after(dve)
    sp.dma(gm_dram, gmT[:], "gmst")

    for tb in range(NFT):
        t0 = tb * FTB
        for oc in range(NC):
            pe.after(dve)
            pe.op("matmul", ps0[:], bdall[:, oc * 128:(oc + 1) * 128],
                  gmT[:, t0:t0 + FTB], start=True, stop=True)
            dve.after(pe).wait_dma("fst")
            dve.op("tensor_copy", ev[:], ps0[:])
            sp.after(dve)
            sp.dma(f_out[oc * 128:(oc + 1) * 128, t0:t0 + FTB], ev[:], "fst")

    if layer == 0:
        groups = [dict(up=p.tensors["w_up"][e // 2],
                       bup=p.tensors["b_up_r"][e // 2],
                       dn=p.tensors["w_down"][e], idx=e, gm=True,
                       upc=caches["up"][e // 2] if caches else None,
                       dnc=caches["dn"][e] if caches else None)
                  for e in range(E)]
    else:
        groups = [dict(up=p.tensors["w1"][e], bup=p.tensors["b1_r"][e],
                       dn=p.tensors["w2"][e], idx=e, gm=True,
                       upc=caches["up"][e] if caches else None,
                       dnc=caches["dn"][e] if caches else None)
                  for e in range(E)]
        groups.append(dict(up=p.tensors["ws1"], bup=p.tensors["bs1_r"][0],
                           dn=p.tensors["ws2"], idx=E, gm=False,
                           upc=caches["up"][E] if caches else None,
                           dnc=caches["dn"][E] if caches else None))

    NCI = D // 128
    upb = p.sbuf("ff_upb", [128, NCI * NCF * 128], BF16)
    dnb = p.sbuf("ff_dnb", [128, NCF * D], BF16)
    wf = (p.sbuf("ff_wf", [128, max(D, min(NCF * 128, 2048))], F32)
          if caches is None else None)
    bupc = p.sbuf("ff_bupc", [128, NCF], F32)
    bupr = p.sbuf("ff_bupr", [NCF, 128], F32)
    xb = p.sbuf("ff_xb", [128, 2 * NCI * FTB], BF16)
    midb = p.sbuf("ff_midb", [128, NCF * FTB], BF16)
    gmbc = p.sbuf("ff_gmbc", [128, FTB], F32)
    grow = p.sbuf("ff_grow", [1, FTB], F32)

    last_mid_act = [0, 0]
    upb_free_t = 0
    evac01_t = 0
    evac2_t = 0
    gmbc_t = 0
    for gi, g in enumerate(groups):
        facc_prev = p.val("facc")
        if caches is not None:
            if upb_free_t:
                sp.after("pe_tick", upb_free_t)
            else:
                sp.after(pe)
            sp.dma(upb[:, 0:NCI * NCF * 128], g["upc"], "wld")
            sp.after(pe)
            sp.dma(dnb[:, 0:NCF * D], g["dnc"], "wld")
        else:
            FH = min(F, 2048)
            for c in range(NCI):
                for fh0 in range(0, F, FH):
                    fl = min(FH, F - fh0)
                    sp.after(gp)
                    sp.dma(wf[:, 0:fl], g["up"][c * 128:(c + 1) * 128,
                                                fh0:fh0 + fl], "wld")
                    gp.wait_dma("wld").after(pe)
                    fc0 = fh0 // 128
                    for fc in range(fc0, (fh0 + fl + 127) // 128):
                        lo_ = max(fc * 128 - fh0, 0)
                        hiw = min((fc + 1) * 128, fh0 + fl) - fh0
                        gp.op("tensor_copy",
                              upb[:, (c * NCF + fc) * 128 + (lo_ + fh0 - fc * 128):
                                  (c * NCF + fc) * 128 + (hiw + fh0 - fc * 128)],
                              wf[:, lo_:hiw])
        sp.after(pe).after(dve).after(gp)
        sp.dma(bupr[:], g["bup"], "wld")
        gp.wait_dma("wld")
        gp.op("engine_nop")
        pe.after(gp).after(dve).after(act)
        pe.op("transpose", ps0[:, 0:NCF], bupr[:], idn[0:NCF, 0:NCF])
        dve.after(pe)
        dve.op("tensor_copy", bupc[:], ps0[:, 0:NCF])
        if caches is None:
            for fc in range(NCF):
                fw = min(128, F - fc * 128)
                sp.after(gp)
                sp.dma(wf[0:fw, 0:D], g["dn"][fc * 128:fc * 128 + fw, :], "wld")
                gp.wait_dma("wld").after(pe)
                gp.op("tensor_copy", dnb[0:fw, fc * D:fc * D + D], wf[0:fw, 0:D])

        for tb in range(NFT):
            t0 = tb * FTB
            xsl = tb % 2
            if tb == 0:
                sp.after(pe)
                for c in range(NCI):
                    sp.dma(xb[:, (xsl * NCI + c) * FTB:(xsl * NCI + c + 1) * FTB],
                           xn[c * 128:(c + 1) * 128, t0:t0 + FTB], "xld")
                xld_tot = p.val("xld")
            if g["gm"]:
                sp.after(pe).wait_dma("gmst")
                sp.dma(grow[:], gm_dram[g["idx"], t0:t0 + FTB].unsqueeze(0),
                       "gld")
                pe.wait_dma("gld")
                if evac2_t:
                    pe.after("dve_tick", evac2_t)
                pe.op("matmul", ps[:, 2 * ALN:2 * ALN + FTB], onr[:], grow[:],
                      start=True, stop=True)
                dve.after(pe)
                dve.op("tensor_copy", gmbc[:], ps[:, 2 * ALN:2 * ALN + FTB])
                gmbc_t = dve.t
            pe.wait_dma("xld", xld_tot).wait_dma("wld")
            if tb == 0 or not evac01_t:
                pe.after(dve)
            else:
                pe.after("dve_tick", evac01_t)
            pe.after(gp)
            last_scale = 0
            for fc in range(NCF):
                fw = min(128, F - fc * 128)
                bank = fc % 2
                psu = ps[:, bank * ALN:bank * ALN + FTB]
                if last_mid_act[bank]:
                    pe.after("act_tick", last_mid_act[bank])
                for c in range(NCI):
                    pe.op("matmul", psu[0:fw, :],
                          upb[:, (c * NCF + fc) * 128:(c * NCF + fc) * 128 + fw],
                          xb[:, (xsl * NCI + c) * FTB:(xsl * NCI + c + 1) * FTB],
                          start=(c == 0), stop=(c == NCI - 1))
                if fc == NCF - 1:
                    if tb == NFT - 1:
                        upb_free_t = pe.t
                    if tb + 1 < NFT:
                        sp.after("pe_tick", pe.t)
                        t1_ = (tb + 1) * FTB
                        nsl = (tb + 1) % 2
                        for c in range(NCI):
                            sp.dma(xb[:, (nsl * NCI + c) * FTB:
                                      (nsl * NCI + c + 1) * FTB],
                                   xn[c * 128:(c + 1) * 128, t1_:t1_ + FTB],
                                   "xld")
                        xld_tot = p.val("xld")
                act.after(pe)
                act.op("activation", midb[0:fw, fc * FTB:(fc + 1) * FTB],
                       psu[0:fw, :], afunc, bias=bupc[0:fw, fc:fc + 1])
                last_mid_act[bank] = act.t
                if g["gm"]:
                    gp.after("act_tick", act.t)
                    if fc == 0:
                        gp.after("dve_tick", gmbc_t)
                    gp.op("tensor_mul", midb[0:fw, fc * FTB:(fc + 1) * FTB],
                          midb[0:fw, fc * FTB:(fc + 1) * FTB], gmbc[0:fw, :])
                    last_scale = gp.t
            if g["gm"]:
                pe.after("gp_tick", last_scale)
            else:
                pe.after(act)
            for oc in range(NC):
                for fc in range(NCF):
                    fw = min(128, F - fc * 128)
                    pe.op("matmul", ps[:, oc * ALN:oc * ALN + FTB],
                          dnb[0:fw, fc * D + oc * 128:fc * D + oc * 128 + 128],
                          midb[0:fw, fc * FTB:(fc + 1) * FTB],
                          start=(fc == 0), stop=(fc == NCF - 1))
            for oc in (2, 0, 1) + tuple(range(3, NC)):
                if oc >= NC:
                    continue
                dve.after(pe).wait_dma("facc").wait_dma("fst")
                dve.op("tensor_copy", ev[:], ps[:, oc * ALN:oc * ALN + FTB])
                if oc == 2 or NC <= 2:
                    evac2_t = dve.t
                if oc == 1 or NC == 1:
                    evac01_t = dve.t
                gp.after(dve).wait_dma("facc", facc_prev)
                if gi == 0:
                    gp.wait_dma("fst")
                gp.dma(f_out[oc * 128:(oc + 1) * 128, t0:t0 + FTB], ev[:],
                       "facc", accum_op=ALU.add)

    nc.sync.wait_ge(p.sem("facc"), p.val("facc"))
    nc.sync.wait_ge(p.sem("fst"), p.val("fst"))


def build_ffn(cfg, layer: int):
    p = Prog()
    S, D = cfg["S"], cfg["D"]
    declare_consts(p)
    p.din("c_iden", [128, 128], F32)
    p.din("xn_fm", [D, S], BF16)
    if layer == 0:
        E, F, NG = cfg["E0"], cfg["F0"], cfg["NG"]
        NCF = (F + 127) // 128
        p.din("probs", [S, E], F32)
        p.din("probs_all", [cfg["NTOK"], E], F32)
        p.din("w_up", [NG, D, F], F32)
        p.din("b_up_r", [NG, NCF, 128], F32)
        p.din("w_down", [E, F, D], F32)
        p.din("b_down", [E, D], F32)
    else:
        E, F = cfg["E1"], cfg["HID1"]
        NCF = (F + 127) // 128
        p.din("combine", [S, E], F32)
        p.din("w1", [E, D, F], F32)
        p.din("b1_r", [E, NCF, 128], F32)
        p.din("w2", [E, F, D], F32)
        p.din("b2", [E, D], F32)
        p.din("ws1", [D, F], F32)
        p.din("bs1_r", [1, NCF, 128], F32)
        p.din("ws2", [F, D], F32)
        p.din("bs2", [1, D], F32)
        p.din("c_ones_srow", [1, S], F32)
    p.dout("ffn_fm", [D, S], BF16)
    emit_ffn(p, cfg, layer)
    p.finish()
    return p



def emit_resid(p: Prog, cfg):
    nc = p.nc
    pe, dve, act, gp, sp = p.chains()
    S, D, TB = cfg["S"], cfg["D"], cfg["TB"]
    NC = D // 128
    x = p.tensors["x_fm"]
    a = p.tensors["attn_fm"]
    f = p.tensors["ffn_fm"]
    xo = p.tensors["xo_fm"]
    xb = p.sbuf("rs_x", [128, TB], F32)
    ab = p.sbuf("rs_a", [128, TB], BF16)
    fb = p.sbuf("rs_f", [128, TB], BF16)
    tb_ = p.sbuf("rs_t", [128, TB], F32)
    ob = p.sbuf("rs_o", [128, TB], F32)
    for c in range(NC):
        for t in range(S // TB):
            t0 = t * TB
            sp.after(gp)
            sp.dma(xb[:], x[c * 128:(c + 1) * 128, t0:t0 + TB], "xld")
            sp.dma(ab[:], a[c * 128:(c + 1) * 128, t0:t0 + TB], "xld")
            sp.dma(fb[:], f[c * 128:(c + 1) * 128, t0:t0 + TB], "xld")
            gp.wait_dma("xld").wait_dma("ost")
            gp.op("tensor_add", tb_[:], ab[:], fb[:])
            gp.op("tensor_add", ob[:], tb_[:], xb[:])
            sp.after(gp)
            sp.dma(xo[c * 128:(c + 1) * 128, t0:t0 + TB], ob[:], "ost")
    nc.sync.wait_ge(p.sem("ost"), p.val("ost"))


def build_resid(cfg):
    p = Prog()
    S, D = cfg["S"], cfg["D"]
    p.din("x_fm", [D, S], F32)
    p.din("attn_fm", [D, S], BF16)
    p.din("ffn_fm", [D, S], BF16)
    p.dout("xo_fm", [D, S], F32)
    emit_resid(p, cfg)
    p.finish()
    return p



def build_ln_router(cfg, layer: int):
    p = Prog()
    S, D = cfg["S"], cfg["D"]
    E = cfg["E0"] if layer == 0 else cfg["E1"]
    declare_consts(p)
    p.din("x_fm", [D, S], F32)
    p.din("w_r", [D, E], F32)
    p.din("b_r", [1, E], F32)
    p.dout("xn_fm", [D, S], BF16)
    p.dout("probs" if layer == 0 else "combine", [S, E], F32)
    emit_ln_router(p, cfg, layer)
    p.finish()
    return p



def _fm(x2d):
    return np.ascontiguousarray(x2d.T)


def _lhsT(w_out_in):
    return np.ascontiguousarray(w_out_in.T)


def _swap_halves(wT, HD):
    Din, QD = wT.shape
    w3 = wT.reshape(Din, QD // HD, HD)
    return np.ascontiguousarray(
        np.concatenate([w3[..., HD // 2:], w3[..., :HD // 2]], axis=-1)
        .reshape(Din, QD))


def _pad_bias(b2d, ncf):
    out = np.zeros((b2d.shape[0], ncf * 128), np.float32)
    out[:, :b2d.shape[1]] = b2d
    return out.reshape(b2d.shape[0], ncf, 128)


def _rope_tables(S, HD, theta):
    inv = 1.0 / theta ** (np.arange(0, HD, 2, dtype=np.float32) / HD)
    t = np.arange(S, dtype=np.float32)
    j = np.arange(128) % HD
    f = inv[(j % (HD // 2))]
    ang = t[None, :] * f[:, None]
    return (np.ascontiguousarray(np.cos(ang).astype(BF)),
            np.ascontiguousarray(np.sin(ang).astype(BF)))



def _forward(inputs, cfg, run):
    R, S, D, HD = cfg["R"], cfg["S"], cfg["D"], cfg["HD"]
    NCF0 = (cfg["F0"] + 127) // 128
    NCF1 = (cfg["HID1"] + 127) // 128
    consts = const_arrays()
    iden = np.eye(128, dtype=np.float32)
    x = np.asarray(inputs["L0_attn_x"], dtype=np.float32)
    x_fm = [_fm(x[i]) for i in range(R)]

    p = build_ln_router(cfg, 0)
    im = [dict(consts, x_fm=x_fm[i],
               w_r=np.ascontiguousarray(inputs["L0_ffn_Wr"]),
               b_r=np.asarray(inputs["L0_ffn_br"]).reshape(1, -1))
          for i in range(R)]
    r0 = run(p, im)
    xn0 = [r["xn_fm"] for r in r0]
    probs = [np.asarray(r["probs"], np.float32) for r in r0]
    probs_all = np.ascontiguousarray(np.concatenate(probs, axis=0))

    p = build_attn(cfg, 0)
    im = [dict(consts, xn_fm=xn0[i],
               w_q=_lhsT(inputs["L0_attn_Wq"]),
               w_k=_lhsT(inputs["L0_attn_Wk"]),
               w_v=_lhsT(inputs["L0_attn_Wv"]),
               ln_gamma=np.ascontiguousarray(
                   np.asarray(inputs["L0_attn_ln_gamma"]).reshape(D, 1)),
               ln_beta=np.ascontiguousarray(
                   np.asarray(inputs["L0_attn_ln_beta"]).reshape(D, 1)))
          for i in range(R)]
    a0 = [r["attn_fm"] for r in run(p, im)]

    p = build_ffn(cfg, 0)
    im = [dict(consts, c_iden=iden, xn_fm=xn0[i], probs=probs[i],
               probs_all=probs_all,
               w_up=np.ascontiguousarray(inputs["L0_ffn_W_up"]),
               b_up_r=_pad_bias(np.asarray(inputs["L0_ffn_b_up"]), NCF0),
               w_down=np.ascontiguousarray(inputs["L0_ffn_W_down"]),
               b_down=np.ascontiguousarray(inputs["L0_ffn_b_down"]))
          for i in range(R)]
    f0 = [r["ffn_fm"] for r in run(p, im)]

    p = build_resid(cfg)
    im = [dict(x_fm=x_fm[i], attn_fm=a0[i], ffn_fm=f0[i]) for i in range(R)]
    x1 = [r["xo_fm"] for r in run(p, im)]

    p = build_ln_router(cfg, 1)
    im = [dict(consts, x_fm=x1[i],
               w_r=np.ascontiguousarray(inputs["L1_ffn_gate_w"]),
               b_r=np.asarray(inputs["L1_ffn_gate_b"]).reshape(1, -1))
          for i in range(R)]
    r1 = run(p, im)
    xn1 = [r["xn_fm"] for r in r1]
    comb = [np.asarray(r["combine"], np.float32) for r in r1]

    cos2, sin2 = _rope_tables(S, HD, cfg["ROPE_THETA"])
    wqT = _lhsT(inputs["L1_attn_wq"])
    wkuT = _lhsT(inputs["L1_attn_w_k_up"])
    p = build_attn(cfg, 1)
    im = [dict(consts, xn_fm=xn1[i],
               w_q=wqT, w_q2=_swap_halves(wqT, HD),
               w_kvd=_lhsT(inputs["L1_attn_w_kv_down"]),
               w_ku=wkuT, w_ku2=_swap_halves(wkuT, HD),
               w_vu=_lhsT(inputs["L1_attn_w_v_up"]),
               w_o=_lhsT(inputs["L1_attn_wo"]),
               c_cos2=cos2, c_sin2=sin2)
          for i in range(R)]
    a1 = [r["attn_fm"] for r in run(p, im)]

    p = build_ffn(cfg, 1)
    im = [dict(consts, c_iden=iden, xn_fm=xn1[i], combine=comb[i],
               w1=np.ascontiguousarray(inputs["L1_ffn_w1"]),
               b1_r=_pad_bias(np.asarray(inputs["L1_ffn_b1"]), NCF1),
               w2=np.ascontiguousarray(inputs["L1_ffn_w2"]),
               b2=np.ascontiguousarray(inputs["L1_ffn_b2"]),
               ws1=np.ascontiguousarray(inputs["L1_ffn_ws1"]),
               bs1_r=_pad_bias(np.asarray(inputs["L1_ffn_bs1"]).reshape(1, -1),
                               NCF1),
               ws2=np.ascontiguousarray(inputs["L1_ffn_ws2"]),
               bs2=np.ascontiguousarray(
                   np.asarray(inputs["L1_ffn_bs2"]).reshape(1, D)),
               c_ones_srow=np.ones((1, S), np.float32))
          for i in range(R)]
    f1 = [r["ffn_fm"] for r in run(p, im)]

    p = build_resid(cfg)
    im = [dict(x_fm=x1[i], attn_fm=a1[i], ffn_fm=f1[i]) for i in range(R)]
    xo = [np.asarray(r["xo_fm"], np.float32) for r in run(p, im)]

    return np.stack([o.T for o in xo], axis=0)



def build_fused(cfg):
    p = Prog()
    S, D, HD = cfg["S"], cfg["D"], cfg["HD"]
    QD = cfg["H"] * HD
    LAT = cfg["LAT"]
    E0, F0, NG = cfg["E0"], cfg["F0"], cfg["NG"]
    E1, HID1 = cfg["E1"], cfg["HID1"]
    NCF0 = (F0 + 127) // 128
    NCF1 = (HID1 + 127) // 128
    R = cfg["R"]

    declare_consts(p)
    p.din("c_iden", [128, 128], F32)
    p.din("c_cos2", [128, S], BF16)
    p.din("c_sin2", [128, S], BF16)
    p.din("c_ones_srow", [1, S], F32)
    x_in = p.din("x_fm_in", [D, S], F32)
    xo_out = p.dout("out_fm", [D, S], F32)

    io = {}
    io["L0w_r"] = p.din("L0w_r", [D, E0], F32)
    io["L0b_r"] = p.din("L0b_r", [1, E0], F32)
    io["L0w_q"] = p.din("L0w_q", [D, QD], F32)
    io["L0w_k"] = p.din("L0w_k", [D, QD], F32)
    io["L0w_v"] = p.din("L0w_v", [D, QD], F32)
    io["L0gamma"] = p.din("L0gamma", [D, 1], F32)
    io["L0beta"] = p.din("L0beta", [D, 1], F32)
    io["L0w_up"] = p.din("L0w_up", [NG, D, F0], F32)
    io["L0b_up"] = p.din("L0b_up", [NG, NCF0, 128], F32)
    io["L0w_down"] = p.din("L0w_down", [E0, F0, D], F32)
    io["L0b_down"] = p.din("L0b_down", [E0, D], F32)
    io["L1w_r"] = p.din("L1w_r", [D, E1], F32)
    io["L1b_r"] = p.din("L1b_r", [1, E1], F32)
    io["L1w_q"] = p.din("L1w_q", [D, QD], F32)
    io["L1w_q2"] = p.din("L1w_q2", [D, QD], F32)
    io["L1w_kvd"] = p.din("L1w_kvd", [D, LAT], F32)
    io["L1w_ku"] = p.din("L1w_ku", [LAT, QD], F32)
    io["L1w_ku2"] = p.din("L1w_ku2", [LAT, QD], F32)
    io["L1w_vu"] = p.din("L1w_vu", [LAT, QD], F32)
    io["L1w_o"] = p.din("L1w_o", [QD, D], F32)
    io["L1w1"] = p.din("L1w1", [E1, D, HID1], F32)
    io["L1b1"] = p.din("L1b1", [E1, NCF1, 128], F32)
    io["L1w2"] = p.din("L1w2", [E1, HID1, D], F32)
    io["L1b2"] = p.din("L1b2", [E1, D], F32)
    io["L1ws1"] = p.din("L1ws1", [D, HID1], F32)
    io["L1bs1"] = p.din("L1bs1", [1, NCF1, 128], F32)
    io["L1ws2"] = p.din("L1ws2", [HID1, D], F32)
    io["L1bs2"] = p.din("L1bs2", [1, D], F32)

    xn = [p.dtmp(f"i_xn{l}", [D, S], BF16) for l in (0, 1)]
    att = [p.dtmp(f"i_att{l}", [D, S], BF16) for l in (0, 1)]
    ffn = [p.dtmp(f"i_ffn{l}", [D, S], BF16) for l in (0, 1)]
    x1 = p.dtmp("i_x1", [D, S], F32)
    probs0 = p.dtmp("i_probs0", [S, E0], F32)
    probs_all = p.dtmp("i_probs_all", [cfg["NTOK"], E0], F32)
    x_all = p.din("x_all", [R, D, S], F32)
    comb1 = p.dtmp("i_comb1", [S, E1], F32)

    NCI = D // 128
    NCQ = QD // 128
    NCL = LAT // 128

    def cj(name, cols):
        return p.dtmp("wc_" + name, [128, cols], BF16)

    jobs = []

    def addproj(name, srcap, Din, Dout, neg=False):
        cols = (Din // 128) * ((Dout + 127) // 128) * 128
        ap = cj(name, cols)
        jobs.append(dict(kind="proj", src=srcap, dst=ap, cols=cols, Din=Din,
                         Dout=Dout, negate_swap=neg))
        return (ap,)

    def addup(name, srcap, Din, F):
        cols = (Din // 128) * ((F + 127) // 128) * 128
        ap = cj(name, cols)
        jobs.append(dict(kind="up", src=srcap, dst=ap, cols=cols, Din=Din, F=F))
        return ap

    def adddn(name, srcap, F, Dout):
        cols = ((F + 127) // 128) * Dout
        ap = cj(name, cols)
        jobs.append(dict(kind="dn", src=srcap, dst=ap, cols=cols, F=F,
                         Dout=Dout))
        return ap

    ac0 = dict(q=addproj("a0q", io["L0w_q"], D, QD),
               k=addproj("a0k", io["L0w_k"], D, QD),
               v=addproj("a0v", io["L0w_v"], D, QD))
    ac1 = dict(q=(addproj("a1q", io["L1w_q"], D, QD)[0],
                  addproj("a1q2", io["L1w_q2"], D, QD, neg=True)[0]),
               kvd=addproj("a1kvd", io["L1w_kvd"], D, LAT),
               ku=(addproj("a1ku", io["L1w_ku"], LAT, QD)[0],
                   addproj("a1ku2", io["L1w_ku2"], LAT, QD, neg=True)[0]),
               vu=addproj("a1vu", io["L1w_vu"], LAT, QD),
               wo=(addup("a1wo", io["L1w_o"], QD, D),))
    fc0 = dict(up=[addup(f"f0u{g}", io["L0w_up"][g], D, F0)
                   for g in range(NG)],
               dn=[adddn(f"f0d{e}", io["L0w_down"][e], F0, D)
                   for e in range(E0)])
    fc1 = dict(up=[addup(f"f1u{e}", io["L1w1"][e], D, HID1)
                   for e in range(E1)] + [addup("f1us", io["L1ws1"], D, HID1)],
               dn=[adddn(f"f1d{e}", io["L1w2"][e], HID1, D)
                   for e in range(E1)] + [adddn("f1ds", io["L1ws2"], HID1, D)])
    xcache = [cj(f"xa{r}", NCI * S) for r in range(R)]
    for r in range(R):
        jobs.append(dict(kind="x", src=x_all[r], dst=xcache[r],
                         cols=NCI * S, Din=D, F=S))

    T = p.tensors
    for layer in (0, 1):
        Lp = f"L{layer}"
        T["x_fm"] = x_in if layer == 0 else x1
        T["xn_fm"] = xn[layer]
        T["attn_fm"] = att[layer]
        T["ffn_fm"] = ffn[layer]
        T["probs"] = probs0
        T["combine"] = comb1
        T["probs_all"] = probs_all
        T["w_r"] = io[Lp + "w_r"]
        T["b_r"] = io[Lp + "b_r"]
        p.push_scope(f"s{layer}a_")
        emit_ln_router(p, cfg, layer)
        if layer == 0:
            p.push_scope("wc_")
            emit_build_caches(p, cfg, jobs)
            p.barrier()
            p.pop_scope()
        else:
            p.barrier()
        p.pop_scope()
        if layer == 0:
            T["x_all"] = x_all
            p.push_scope("s0g_")
            emit_router_global(p, cfg, xcache=xcache)
            p.barrier()
            p.pop_scope()
        if layer == 0:
            T["w_q"] = io["L0w_q"]
            T["w_k"] = io["L0w_k"]
            T["w_v"] = io["L0w_v"]
            T["ln_gamma"] = io["L0gamma"]
            T["ln_beta"] = io["L0beta"]
        else:
            T["w_q"] = io["L1w_q"]
            T["w_q2"] = io["L1w_q2"]
            T["w_kvd"] = io["L1w_kvd"]
            T["w_ku"] = io["L1w_ku"]
            T["w_ku2"] = io["L1w_ku2"]
            T["w_vu"] = io["L1w_vu"]
            T["w_o"] = io["L1w_o"]
        p.push_scope(f"s{layer}b_")
        emit_attn(p, cfg, layer, caches=(ac0 if layer == 0 else ac1))
        p.barrier()
        p.pop_scope()
        if layer == 0:
            T["w_up"] = io["L0w_up"]
            T["b_up_r"] = io["L0b_up"]
            T["w_down"] = io["L0w_down"]
            T["b_down"] = io["L0b_down"]
        else:
            T["w1"] = io["L1w1"]
            T["b1_r"] = io["L1b1"]
            T["w2"] = io["L1w2"]
            T["b2"] = io["L1b2"]
            T["ws1"] = io["L1ws1"]
            T["bs1_r"] = io["L1bs1"]
            T["ws2"] = io["L1ws2"]
            T["bs2"] = io["L1bs2"]
        p.push_scope(f"s{layer}c_")
        emit_ffn(p, cfg, layer, caches=(fc0 if layer == 0 else fc1))
        p.barrier()
        p.pop_scope()
        T["xo_fm"] = x1 if layer == 0 else xo_out
        p.push_scope(f"s{layer}d_")
        emit_resid(p, cfg)
        p.barrier()
        p.pop_scope()
    p.finish()
    return p


def _fused_in_map(inputs, cfg, core_x_fm):
    D, S, HD = cfg["D"], cfg["S"], cfg["HD"]
    NCF0 = (cfg["F0"] + 127) // 128
    NCF1 = (cfg["HID1"] + 127) // 128
    cos2, sin2 = _rope_tables(S, HD, cfg["ROPE_THETA"])
    wqT = _lhsT(inputs["L1_attn_wq"])
    wkuT = _lhsT(inputs["L1_attn_w_k_up"])
    m = dict(const_arrays())
    m.update(
        c_iden=np.eye(128, dtype=np.float32),
        c_cos2=cos2, c_sin2=sin2,
        c_ones_srow=np.ones((1, S), np.float32),
        x_fm_in=core_x_fm,
        x_all=inputs["_x_all_fm"],
        L0w_r=np.ascontiguousarray(inputs["L0_ffn_Wr"]),
        L0b_r=np.asarray(inputs["L0_ffn_br"]).reshape(1, -1),
        L0w_q=_lhsT(inputs["L0_attn_Wq"]),
        L0w_k=_lhsT(inputs["L0_attn_Wk"]),
        L0w_v=_lhsT(inputs["L0_attn_Wv"]),
        L0gamma=np.ascontiguousarray(
            np.asarray(inputs["L0_attn_ln_gamma"]).reshape(D, 1)),
        L0beta=np.ascontiguousarray(
            np.asarray(inputs["L0_attn_ln_beta"]).reshape(D, 1)),
        L0w_up=np.ascontiguousarray(inputs["L0_ffn_W_up"]),
        L0b_up=_pad_bias(np.asarray(inputs["L0_ffn_b_up"]), NCF0),
        L0w_down=np.ascontiguousarray(inputs["L0_ffn_W_down"]),
        L0b_down=np.ascontiguousarray(inputs["L0_ffn_b_down"]),
        L1w_r=np.ascontiguousarray(inputs["L1_ffn_gate_w"]),
        L1b_r=np.asarray(inputs["L1_ffn_gate_b"]).reshape(1, -1),
        L1w_q=wqT, L1w_q2=_swap_halves(wqT, HD),
        L1w_kvd=_lhsT(inputs["L1_attn_w_kv_down"]),
        L1w_ku=wkuT, L1w_ku2=_swap_halves(wkuT, HD),
        L1w_vu=_lhsT(inputs["L1_attn_w_v_up"]),
        L1w_o=_lhsT(inputs["L1_attn_wo"]),
        L1w1=np.ascontiguousarray(inputs["L1_ffn_w1"]),
        L1b1=_pad_bias(np.asarray(inputs["L1_ffn_b1"]), NCF1),
        L1w2=np.ascontiguousarray(inputs["L1_ffn_w2"]),
        L1b2=np.ascontiguousarray(inputs["L1_ffn_b2"]),
        L1ws1=np.ascontiguousarray(inputs["L1_ffn_ws1"]),
        L1bs1=_pad_bias(np.asarray(inputs["L1_ffn_bs1"]).reshape(1, -1), NCF1),
        L1ws2=np.ascontiguousarray(inputs["L1_ffn_ws2"]),
        L1bs2=np.ascontiguousarray(
            np.asarray(inputs["L1_ffn_bs2"]).reshape(1, D)),
    )
    return m


def _forward_fused(inputs, cfg, run):
    R = cfg["R"]
    x = np.asarray(inputs["L0_attn_x"], dtype=np.float32)
    p = build_fused(cfg)
    x_all = np.ascontiguousarray(np.stack([_fm(x[i]) for i in range(R)]))
    inputs = dict(inputs, _x_all_fm=x_all)
    im = [_fused_in_map(inputs, cfg, x_all[i]) for i in range(R)]
    res = run(p, im)
    return np.stack([np.asarray(r["out_fm"], np.float32).T for r in res],
                    axis=0)


def _hw_run(prog, in_maps):
    res = run_bass_kernel_spmd(prog.nc, in_maps,
                               core_ids=list(range(len(in_maps))))
    return res.results


def kernel(**inputs):
    cfg = default_cfg()
    return _forward_fused(inputs, cfg, _hw_run)
